# version 16
# baseline (speedup 1.0000x reference)
import contextlib
import os
import numpy as np
import ml_dtypes
import concourse.bass as bass
import concourse.mybir as mybir
from concourse.bass_utils import run_bass_kernel_spmd

F32 = mybir.dt.float32
BF16 = mybir.dt.bfloat16
I32 = mybir.dt.int32
AF = mybir.ActivationFunctionType
ALU = mybir.AluOpType
AX = mybir.AxisListType

P = 128
D = 1024
H = 1536
G = 3072
TT = 512
SEQ = 2048
NSEQ = 2
N_TILES = (SEQ // TT) * NSEQ
TC = TT // P
DC = D // P
HC = H // P
GC = G // P
GRP = 6
EPS_LN = 1e-5
EPS_B = 1e-6

ENGINES = ["sync", "scalar", "vector", "tensor", "gpsimd"]


class Prog:

    RISKY = ("tensor_reduce", "reciprocal", "tensor_tensor_scan", "bn_stats")

    def __init__(self):
        self.ops = {e: [] for e in ENGINES}
        self.val = {e: 0 for e in ENGINES}
        self.cval = {}
        self.waited = {c: {} for c in ENGINES}
        self.risky = set()

    def op(self, eng, method, *args, waits=(), dma=False, sem=None,
           inc=None, **kwargs):
        if dma:
            assert sem is not None, "dma ops need a channel sem"
            self.cval[sem] = self.cval.get(sem, 0) + 16
            key, val_after = sem, self.cval[sem]
            inc = 16
        else:
            assert sem is None
            if inc is None:
                inc = 1
            self.val[eng] += inc
            key, val_after = eng, self.val[eng]
        wl = {}
        for tok in waits:
            if tok is None:
                continue
            wkey, wval = tok
            wl[wkey] = max(wl.get(wkey, 0), wval)
        fw = []
        for wkey, wval in wl.items():
            if wkey == eng:
                if wval > self.val[eng]:
                    raise RuntimeError(f"self-wait in future: {eng} {wval}")
                fw.append((wkey, wval))
            elif wval > self.waited[eng].get(wkey, 0):
                fw.append((wkey, wval))
                self.waited[eng][wkey] = wval
        self.ops[eng].append((method, args, kwargs, fw, inc if not dma else 16,
                              key if (dma or inc) else None))
        if method in self.RISKY or kwargs.get("accum_out") is not None:
            self.risky.add((key, val_after))
        return (key, val_after)

    def wait(self, eng, waits):
        return self.op(eng, "__wait__", waits=waits, inc=0)

    def milestone(self, eng):
        return (eng, self.val[eng])

    def fence(self, ch):
        return (ch, self.cval.get(ch, 0))

    def build(self, nc, stack, simsafe=False):
        ops2 = {}
        for e in ENGINES:
            lst = []
            riskydone = 0
            for method, args, kwargs, waits, inc, key in self.ops[e]:
                if not simsafe:
                    keepall = e in os.environ.get("KM_SELF", "vector")
                    w2 = []
                    for wk, wv in waits:
                        if wk != e:
                            w2.append((wk, wv))
                        elif keepall or ((e, wv) in self.risky and wv > riskydone):
                            w2.append((wk, wv))
                            riskydone = max(riskydone, wv)
                    waits = w2
                    if method == "__wait__" and not waits:
                        continue
                lst.append([method, args, kwargs, waits, inc, key])
            ops2[e] = lst
        if not simsafe:
            referenced = {e: set() for e in ENGINES}
            for e in ENGINES:
                for op in ops2[e]:
                    for wk, wv in op[3]:
                        if wk in referenced:
                            referenced[wk].add(wv)
            remap = {}
            for e in ENGINES:
                m = {}
                orig = 0
                new = 0
                for op in ops2[e]:
                    method, args, kwargs, waits, inc, key = op
                    if inc and key == e:
                        orig += inc
                        if orig in referenced[e]:
                            new += 1
                            op[4] = 1
                        else:
                            op[4] = 0
                        m[orig] = new
                remap[e] = m
            for e in ENGINES:
                for op in ops2[e]:
                    op[3] = [(wk, remap[wk][wv] if wk in remap else wv)
                             for wk, wv in op[3]]
        keys = list(ENGINES) + sorted(self.cval.keys())
        sems = {k: stack.enter_context(nc.semaphore(f"s_{k}")) for k in keys}
        block = stack.enter_context(nc.Block())
        for e in ENGINES:
            ops = ops2[e]
            if not ops:
                continue

            def make_runner(eng_name, eng_ops):
                def runner(engine):
                    for method, args, kwargs, waits, inc, key in eng_ops:
                        for wkey, wval in waits:
                            engine.wait_ge(sems[wkey], wval)
                        if method == "__wait__":
                            continue
                        inst = getattr(engine, method)(*args, **kwargs)
                        if inc and key is not None:
                            inst.then_inc(sems[key], inc)
                return runner

            getattr(block, e)(make_runner(e, ops))


def build_nc(n_tiles=N_TILES, debug=False):
    nc = bass.Bass("TRN2", target_bir_lowering=False)

    def dI(name, shape, dt=F32):
        return nc.dram_tensor(name, shape, dt, kind="ExternalInput").ap()

    xt = dI("xt", [NSEQ, D, SEQ])
    xtok_d = dI("xtok", [NSEQ * SEQ, D])
    wiT = dI("wiT", [D, G])
    wgT = dI("wgT", [H, G])
    woT = dI("woT", [H, D])
    cw_d = dI("cw", [P, 4 * HC])
    cb_d = dI("cb", [P, HC])
    fb_d = dI("fb", [P, HC])
    bg_d = dI("bg", [P, 2 * HC])
    fw_d = dI("fw", [D, D])
    fbias_d = dI("fbias", [1, D])
    identb_d = dI("identb", [P, P], BF16)
    out_d = nc.dram_tensor("out", [NSEQ * SEQ, D], F32, kind="ExternalOutput").ap()

    wiC = nc.dram_tensor("wiC", [GC, P, DC * P], BF16, kind="Internal").ap()
    wdC = nc.dram_tensor("wdC", [DC, P, D], BF16, kind="Internal").ap()
    wgC = nc.dram_tensor("wgC", [GC, P, HC * P], BF16, kind="Internal").ap()
    woC = nc.dram_tensor("woC", [HC, P, D], BF16, kind="Internal").ap()

    stack = contextlib.ExitStack()

    def sb(name, shape, dt=F32):
        return stack.enter_context(nc.sbuf_tensor(name, shape, dt))

    def psum(name, shape, dt=F32):
        return stack.enter_context(nc.psum_tensor(name, shape, dt))

    with stack:
        xTb = sb("xTb", [P, DC, TT], BF16)
        xtok = sb("xtok_s", [P, TC, D])
        NPB = 3
        wi_b = sb("wi_b", [P, NPB, DC, P], BF16)
        wg_b = sb("wg_b", [P, NPB, HC, P], BF16)
        wo_b = sb("wo_b", [P, NPB, D], BF16)
        stg32 = sb("stg32", [P, HC * P])
        gg = sb("gg", [P, HC, TT], BF16)
        xh = sb("xh", [P, HC, TT], BF16)
        ghb = sb("ghb", [P, HC, TT], BF16)
        bufA = sb("bufA", [P, GRP, TT])
        bufB = sb("bufB", [P, GRP, TT])
        bufC = sb("bufC", [P, GRP, TT])
        convs = sb("convs", [P, 2, 3 + TT])
        convacc = sb("convacc", [P, 1, TT])
        tailb = sb("tailb", [P, HC, 3])
        carry = sb("carry", [P, HC])
        cw = sb("cw_s", [P, 4 * HC])
        cb = sb("cb_s", [P, HC])
        fbs = sb("fbs_s", [P, HC])
        chalf = sb("chalf_s", [P, HC])
        bgh = sb("bgh_s", [P, 2 * HC])
        fbias = sb("fbias_s", [1, D])
        identb = sb("identb_s", [P, P], BF16)
        ones1 = sb("ones1", [1, P])
        y = sb("y", [P, TC, D])
        qa = sb("qa", [P, D])
        qb = sb("qb", [P, D])
        qd = sb("qd", [P, D])
        qi = sb("qi", [P, D], I32)
        qf = qi
        m8 = sb("m8", [P, 32])
        scl = sb("scl", [P, 32])
        invs = sb("invs", [P, 32])
        xd_tok = sb("xd_tok", [P, TC, D], BF16)
        xd_feat = sb("xd_feat", [P, DC, TT], BF16)
        wdT = sb("wdT", [P, DC, D], BF16)
        wd_p = sb("wd_p", [P, D], BF16)
        wdcol = sb("wdcol", [P, DC, P], BF16)
        outstg = sb("outstg", [P, 1, D])
        qc = outstg[:, 0, :]
        smalls = sb("smalls", [P, 32])
        det = sb("det", [1, 4])
        deti = sb("deti", [1, 1], I32)
        offp = sb("offp", [P, 1])

        pbf = [psum(f"pbf{i}", [P, TT]) for i in range(3)]
        pWo = psum("pWo", [P, D])
        pF = psum("pF", [P, D])
        pT = psum("pT", [P, D], BF16)

        K = Prog()
        S = {}
        S["outstg_read"] = [None]

        pbfW = [None, None, None, None, None]
        pbfR = [None, None, None, None, None]
        pWoW = [None]
        pWoR = [None]
        pFW = [None]
        pFR = [None]
        pTW = [None]
        pTR = [None]

        for _dst, _src in [(cw[:], cw_d), (cb[:], cb_d), (fbs[:], fb_d),
                           (bgh[:], bg_d), (fbias[:], fbias_d),
                           (identb[:], identb_d)]:
            K.op("sync", "dma_start", out=_dst, in_=_src, dma=True,
                 sem="ld_misc")
        misc_f = K.fence("ld_misc")
        l_cw = l_cb = l_fb = l_bg = l_fbias = l_identb = misc_f

        t_abs = K.op("scalar", "activation", out=smalls[:, 0:12], in_=fbs[:],
                     func=AF.Abs, waits=[l_fb])
        t_e = K.op("scalar", "activation", out=smalls[:, 0:12],
                   in_=smalls[:, 0:12], func=AF.Exp, scale=-1.0, waits=[t_abs])
        t_l = K.op("scalar", "activation", out=smalls[:, 0:12],
                   in_=smalls[:, 0:12], func=AF.Ln, bias=1.0, waits=[t_e])
        t_r = K.op("vector", "tensor_scalar", out=chalf[:], in0=fbs[:],
                   scalar1=0.0, scalar2=None, op0=ALU.max, waits=[l_fb])
        t_sp = K.op("vector", "tensor_tensor", out=chalf[:], in0=chalf[:],
                    in1=smalls[:, 0:12], op=ALU.add, waits=[t_l, t_r])
        t_ch = K.op("vector", "tensor_scalar", out=chalf[:], in0=chalf[:],
                    scalar1=-4.0, scalar2=None, op0=ALU.mult, waits=[t_sp])
        t_bgh = K.op("vector", "tensor_scalar", out=bgh[:], in0=bgh[:],
                     scalar1=0.5, scalar2=None, op0=ALU.mult, waits=[l_bg])
        t_ones = K.op("vector", "memset", ones1[:], 1.0)
        t_eps = K.op("vector", "memset", smalls[:, 20:21], EPS_LN)
        t_d0 = K.op("vector", "memset", det[:, 0:1], 67.5)
        t_d1 = K.op("vector", "tensor_scalar", out=deti[:], in0=det[:, 0:1],
                    scalar1=1.0, scalar2=None, op0=ALU.mult, waits=[t_d0])
        t_d2 = K.op("vector", "tensor_scalar", out=det[:, 1:2], in0=deti[:],
                    scalar1=-0.5, scalar2=98.0, op0=ALU.mult, op1=ALU.add,
                    waits=[t_d1])
        t_off_mm = K.op("tensor", "matmul", pbf[0][:, 0:1], ones1[:],
                        det[:, 1:2], start=True, stop=True,
                        waits=[t_ones, t_d2])
        pbfW[0] = t_off_mm
        t_off = K.op("vector", "tensor_copy", out=offp[:], in_=pbf[0][:, 0:1],
                     waits=[t_off_mm])
        pbfR[0] = t_off

        def emit_quant(y_ap, out_bf_ap, pre_waits, extra_out_waits=()):
            w = [t for t in pre_waits if t is not None]
            if S.get("quant_prev"):
                w.append(S["quant_prev"])
            t_m8 = K.op("vector", "tensor_reduce", out=m8[:],
                        in_=y_ap.rearrange("p (b k) -> p b k", k=32),
                        axis=AX.X, op=ALU.max, apply_absolute_value=True,
                        waits=w)
            t_sc = K.op("vector", "tensor_scalar", out=scl[:], in0=m8[:],
                        scalar1=1.0 / 7.5, scalar2=1e-30, op0=ALU.mult,
                        op1=ALU.max, waits=[t_m8])
            t_iv = K.op("vector", "reciprocal", out=invs[:], in_=scl[:],
                        waits=[t_sc])
            ivb = invs[:].unsqueeze(2).broadcast_to([P, 32, 32])
            t_u8 = K.op("vector", "scalar_tensor_tensor",
                        out=qa[:].rearrange("p (b k) -> p b k", k=32),
                        in0=y_ap.rearrange("p (b k) -> p b k", k=32),
                        scalar=8.0, in1=ivb, op0=ALU.mult, op1=ALU.mult,
                        waits=[t_iv])
            t_eb = K.op("vector", "tensor_scalar", out=qi[:],
                        in0=qa[:].bitcast(I32), scalar1=0x7F800000,
                        scalar2=None, op0=ALU.bitwise_and, waits=[t_u8])
            ebf = qi[:].bitcast(F32)
            t_c1 = K.op("vector", "tensor_scalar", out=qb[:], in0=ebf,
                        scalar1=16.0, scalar2=-0.5, op0=ALU.is_ge,
                        op1=ALU.mult, waits=[t_eb])
            t_c2 = K.op("vector", "tensor_scalar", out=qc, in0=ebf,
                        scalar1=32.0, scalar2=-0.25, op0=ALU.is_ge,
                        op1=ALU.mult,
                        waits=[t_eb, S["outstg_read"][0] if S.get("outstg_read") else None])
            t_ig = K.op("vector", "scalar_tensor_tensor", out=qd[:], in0=qb[:],
                        scalar=1.0, in1=qc, op0=ALU.add, op1=ALU.add,
                        waits=[t_c1, t_c2])
            t_w = K.op("vector", "tensor_tensor", out=qa[:], in0=qa[:],
                       in1=qd[:], op=ALU.mult, waits=[t_ig])
            t_fi = K.op("vector", "tensor_scalar", out=qi[:], in0=qa[:],
                        scalar1=offp[:], scalar2=None, op0=ALU.add,
                        waits=[t_w, t_off])
            t_z2 = K.op("vector", "tensor_scalar", out=qd[:], in0=qc,
                        scalar1=-1.0, scalar2=0.125, op0=ALU.mult, op1=ALU.add,
                        waits=[t_ig, t_w])
            t_st = K.op("vector", "scalar_tensor_tensor", out=qb[:], in0=qb[:],
                        scalar=-0.25, in1=qd[:], op0=ALU.mult, op1=ALU.add,
                        waits=[t_z2])
            sclb = scl[:].unsqueeze(2).broadcast_to([P, 32, 32])
            t_ss = K.op("vector", "tensor_tensor",
                        out=qd[:].rearrange("p (b k) -> p b k", k=32),
                        in0=qb[:].rearrange("p (b k) -> p b k", k=32),
                        in1=sclb, op=ALU.mult, waits=[t_st])
            t_xd = K.op("vector", "scalar_tensor_tensor", out=out_bf_ap,
                        in0=qi[:], scalar=-64.0, in1=qd[:],
                        op0=ALU.add, op1=ALU.mult,
                        waits=[t_fi, t_ss] + [t for t in extra_out_waits if t])
            S["quant_prev"] = t_xd
            return t_xd

        S["fw_jc"] = 0
        S["fw_slot_read"] = [None, None]
        S["fw_dma_prev"] = None

        def pump_fw():
            jc = S["fw_jc"]
            if jc >= DC:
                return
            S["fw_jc"] = jc + 1
            sl = jc % 2
            w = [S["fw_slot_read"][sl]] if S["fw_slot_read"][sl] else []
            l = K.op("sync", "dma_start", out=y[:, sl, :],
                     in_=fw_d.rearrange("(c p) d -> c p d", p=P)[jc],
                     waits=w, dma=True, sem="ld_fw")
            t_q = emit_quant(y[:, sl, :], wd_p[:],
                             [l] + ([S["fw_dma_prev"]] if S["fw_dma_prev"] else []))
            S["fw_slot_read"][sl] = t_q
            S["fw_dma_prev"] = K.op("scalar", "dma_start", out=wdC[jc],
                                    in_=wd_p[:], waits=[t_q], dma=True,
                                    sem="st_wd")

        def emit_wd_transposes():
            t_evac = None
            t_col = None
            t_col_done = None
            wd_fence = K.fence("st_wd")
            for dc in range(DC):
                w = [wd_fence]
                if t_col is not None:
                    w.append(t_col_done)
                t_col = K.op("sync", "dma_start", out=wdcol[:],
                             in_=wdC.rearrange("j p d -> p j d")[:, :, dc * P:(dc + 1) * P],
                             waits=w, dma=True, sem="ld_wdcol")
                tr_last = None
                for jc in range(DC):
                    w2 = [t_col, l_identb]
                    if pTR[0] is not None:
                        w2.append(pTR[0])
                    tr_last = K.op("tensor", "transpose",
                                   pT[:, jc * P:(jc + 1) * P],
                                   wdcol[:, jc, :],
                                   identb[:], waits=w2)
                pTW[0] = tr_last
                t_col_done = tr_last
                t_evac = K.op("scalar", "copy", out=wdT[:, dc, :], in_=pT[:],
                              waits=[tr_last])
                pTR[0] = t_evac
            S["t_wdT"] = t_evac
            S["y_read"] = [S["fw_slot_read"][0], S["fw_slot_read"][1],
                           None, None]

        S["xTb_read"] = None
        S["xtstg_read"] = None
        S["xtok_read"] = []
        S["wi_read"] = [None, None, None]
        S["wg_read"] = [None, None, None]
        S["wo_read"] = [None, None, None]
        S["wi_cached"] = [None] * GC
        S["wg_cached"] = [None] * GC
        S["wo_cached"] = [None] * HC
        S["gg_read"] = [None] * HC
        S["xh_read"] = [None] * HC
        S["conv_read"] = [None, None]
        S["convacc_read"] = [None, None]
        S["abc_read"] = [None] * GRP
        S["carry_tok"] = [None] * HC
        S["tail_tok"] = [None] * HC
        S["quant_toks"] = None
        S["y_read"] = [None] * TC
        S["s_toks"] = []
        S["ghb_read"] = [None] * HC
        S["xd_tok_read"] = None
        S["qa_read"] = None
        S["out_toks"] = []
        S["prev_g"] = None
        S["xd_feat_read"] = None
        S["xd_feat_w"] = [None] * TC
        S["wo_mm"] = [None] * TC
        S["l_xtok"] = None

        def load_piece(kind, idx, ch, dst_ap, cacheC, src_fn, read_tok,
                       use_cache):
            w = [read_tok] if read_tok else []
            if dst_ap.ndim == 3:
                cols = dst_ap.shape[-1] * dst_ap.shape[-2]
                dst2 = dst_ap.rearrange("p a b -> p (a b)")
            else:
                cols = dst_ap.shape[-1]
                dst2 = dst_ap
            if not use_cache:
                w.append(K.fence(f"st_{kind}"))
                lw = [S["stg_read"]] if S["stg_read"] else []
                stg = stg32[:, 0:cols]
                stg_in = (stg.rearrange("p (a b) -> p a b", b=dst_ap.shape[-1])
                          if dst_ap.ndim == 3 else stg)
                l = K.op("scalar" if kind == "wo" else "sync", "dma_start",
                         out=stg_in, in_=src_fn(),
                         waits=w + lw, dma=True, sem="ld_stg")
                c = K.op("vector", "tensor_copy", out=dst2, in_=stg,
                         waits=[l] + w)
                S["stg_read"] = c
                st = K.op("scalar", "dma_start", out=cacheC[idx], in_=dst2,
                          waits=[c], dma=True, sem=f"st_{kind}")
                S[f"{kind}_cached"][idx] = st
                return c
            else:
                w.append(K.fence(f"st_{kind}"))
                return K.op("sync", "dma_start", out=dst2, in_=cacheC[idx],
                            waits=w, dma=True, sem=ch)

        def emit_ffn_prev(g):
            pg = g - 1
            out_toks = []

            def emit_xdt(tc):
                tr_last = None
                for dc in range(DC):
                    w = [S[f"xd_w_{tc}"], l_identb]
                    if dc == 0 and pTR[0] is not None:
                        w.append(pTR[0])
                    tr_last = K.op("tensor", "transpose",
                                   pT[:, dc * P:(dc + 1) * P],
                                   xd_tok[:, tc, dc * P:(dc + 1) * P],
                                   identb[:], waits=w)
                pTW[0] = tr_last
                wv = [S["xd_feat_read"]] if (tc == 0 and S["xd_feat_read"]) else []
                t_ev = K.op("scalar", "copy",
                            out=xd_feat[:, 0:DC, tc * P:(tc + 1) * P],
                            in_=pT[:].rearrange("p (a b) -> p a b", a=DC),
                            waits=[tr_last] + wv)
                pTR[0] = t_ev
                S["xd_feat_w"][tc] = t_ev

            emit_xdt(0)
            emit_back_fin(2)
            emit_xdt(1)
            for tc in range(TC):
                if tc == 1:
                    emit_back_fin(3)
                if tc >= 2:
                    emit_xdt(tc)
                mm = None
                for dc in range(DC):
                    w = [S["xd_feat_w"][tc], S["t_wdT"]]
                    if dc == 0 and pFR[0] is not None:
                        w.append(pFR[0])
                    mm = K.op("tensor", "matmul", pF[:, 0:TT],
                              xd_feat[:, dc, tc * P:(tc + 1) * P],
                              wdT[:, dc, 0:TT],
                              start=(dc == 0), stop=False,
                              waits=w if dc == 0 else [], inc=0)
                    mm = K.op("tensor", "matmul", pF[:, TT:D],
                              xd_feat[:, dc, tc * P:(tc + 1) * P],
                              wdT[:, dc, TT:D],
                              start=(dc == 0), stop=False, inc=0)
                K.op("tensor", "matmul", pF[:, 0:TT], ones1[:],
                     fbias[:, 0:TT], start=False, stop=True,
                     waits=[l_fbias], inc=0)
                t_mm = K.op("tensor", "matmul", pF[:, TT:D], ones1[:],
                            fbias[:, TT:D], start=False, stop=True)
                pFW[0] = t_mm
                ob = 0
                wv = [S["outstg_read"][ob]] if S["outstg_read"][ob] else []
                t_v = K.op("vector", "scalar_tensor_tensor",
                           out=outstg[:, ob, :], in0=pF[:], scalar=1.0,
                           in1=y[:, tc, :], op0=ALU.mult, op1=ALU.add,
                           accum_out=smalls[:, 21:22],
                           waits=[t_mm, S[f"y_w_{tc}"], S.get("fp_prev"),
                                  S.get("quant_prev")] + wv)
                pFR[0] = t_v
                S["y_read"][tc] = t_v
                wq = [S["qa_read"]] if S["qa_read"] else []
                t_sq = K.op("scalar", "activation", out=qa[:],
                            in_=outstg[:, ob, :], func=AF.Square,
                            accum_out=smalls[:, 22:23], waits=[t_v] + wq)
                S["qa_read"] = t_sq
                t_m = K.op("vector", "tensor_scalar", out=smalls[:, 23:24],
                           in0=smalls[:, 21:22], scalar1=1.0 / D, scalar2=None,
                           op0=ALU.mult, waits=[t_v])
                t_m2 = K.op("vector", "tensor_tensor", out=smalls[:, 25:26],
                            in0=smalls[:, 23:24], in1=smalls[:, 23:24],
                            op=ALU.mult, waits=[t_m])
                t_var = K.op("vector", "scalar_tensor_tensor",
                             out=smalls[:, 26:27], in0=smalls[:, 22:23],
                             scalar=1.0 / D, in1=smalls[:, 25:26],
                             op0=ALU.mult, op1=ALU.subtract,
                             waits=[t_sq, t_m2])
                t_sd = K.op("scalar", "activation", out=smalls[:, 27:28],
                            in_=smalls[:, 26:27], func=AF.Sqrt,
                            bias=smalls[:, 20:21], waits=[t_var, t_eps])
                t_ri = K.op("vector", "reciprocal", out=smalls[:, 28:29],
                            in_=smalls[:, 27:28], waits=[t_sd])
                t_o = K.op("vector", "tensor_scalar", out=outstg[:, ob, :],
                           in0=outstg[:, ob, :], scalar1=smalls[:, 23:24],
                           scalar2=smalls[:, 28:29], op0=ALU.subtract,
                           op1=ALU.mult, waits=[t_ri, t_sq])
                t_dma = K.op("scalar", "dma_start",
                             out=out_d.rearrange("(n c p) d -> n c p d",
                                                 c=TC, p=P)[pg, tc],
                             in_=outstg[:, ob, :], waits=[t_o], dma=True,
                             sem="st_out")
                S["outstg_read"][ob] = t_dma
                S["fp_prev"] = t_o
                out_toks.append(t_dma)
            S["out_toks"] += out_toks
            S["have_back"] = False
            S["xd_tok_read"] = pTW[0]
            S["xd_feat_read"] = K.milestone("tensor")

        def emit_scan_group(g, g0, with_sd=False):
            for j in range(GRP):
                gi = g0 + j
                t_b = K.op("scalar", "activation", out=bufC[:, j, :],
                           in_=bufC[:, j, :], func=AF.Sqrt,
                           waits=[S[f"chain_{gi}"][0]])
                t_bsq, t_xp, t_al = S[f"chain_{gi}"]
                t_xg = K.op("vector", "scalar_tensor_tensor",
                            out=bufB[:, j, :], in0=bufC[:, j, :], scalar=0.5,
                            in1=bufB[:, j, :], op0=ALU.mult, op1=ALU.mult,
                            waits=[t_b, t_xp])
                t_h = K.op("vector", "tensor_tensor_scan", out=bufC[:, j, :],
                           data0=bufA[:, j, :], data1=bufB[:, j, :],
                           initial=carry[:, gi:gi + 1],
                           op0=ALU.mult, op1=ALU.add,
                           waits=[t_xg, t_al, S["carry_tok"][gi]])
                S[f"h_{gi}"] = t_h
                wv = [S["ghb_read"][gi]] if S["ghb_read"][gi] else []
                t_gh = K.op("vector", "tensor_tensor", out=ghb[:, gi, :],
                            in0=gg[:, gi, :], in1=bufC[:, j, :], op=ALU.mult,
                            waits=[t_h, S[f"gg_w_{gi}"]] + wv)
                S[f"gh_w_{gi}"] = t_gh
                S["abc_read"][j] = t_gh
                S["gg_read"][gi] = t_gh
            t_c = K.op("vector", "tensor_copy", out=carry[:, g0:g0 + GRP],
                       in_=bufC[:, :, TT - 1:TT].rearrange("p a b -> p (a b)"),
                       waits=[S[f"h_{g0 + j}"] for j in range(GRP)])
            for j in range(GRP):
                S["carry_tok"][g0 + j] = t_c
                S["abc_read"][j] = [S["abc_read"][j], t_c]
            if with_sd:
                for tc in range(TC):
                    emit_back_sd(tc)

        def emit_back_s(g, tc, pbank):
            t_s = K.op("vector", "scalar_tensor_tensor", out=y[:, tc, :],
                       in0=pbank[:], scalar=1.0, in1=xtok[:, tc, :],
                       op0=ALU.mult, op1=ALU.add,
                       accum_out=smalls[:, tc:tc + 1],
                       waits=[S["wo_mm"][tc], S["l_xtok"],
                              S["y_read"][tc]])
            if pbank is pWo:
                pWoR[0] = t_s
            else:
                pFR[0] = t_s
            S["s_toks"].append(t_s)
            S[f"s_done_{tc}"] = t_s

        def emit_back_stats(tc):
            wq = [S["qa_read"]] if S["qa_read"] else []
            t_s = S[f"s_done_{tc}"]
            t_sq = K.op("scalar", "activation", out=qa[:], in_=y[:, tc, :],
                        func=AF.Square, accum_out=smalls[:, 4 + tc:5 + tc],
                        waits=[t_s] + wq
                        + ([S["st_prev"]] if S.get("st_prev") else []))
            S["qa_read"] = t_sq
            S[f"sq_{tc}"] = t_sq
            if tc == TC - 1:
                t_m = K.op("vector", "tensor_scalar", out=smalls[:, 8:12],
                           in0=smalls[:, 0:4], scalar1=1.0 / D, scalar2=None,
                           op0=ALU.mult,
                           waits=[S[f"s_done_{c}"] for c in range(TC)]
                           + [S.get("st_prev")])
                t_m2 = K.op("vector", "tensor_tensor", out=smalls[:, 24:28],
                            in0=smalls[:, 8:12], in1=smalls[:, 8:12],
                            op=ALU.mult, waits=[t_m])
                t_var = K.op("vector", "scalar_tensor_tensor",
                             out=smalls[:, 12:16], in0=smalls[:, 4:8],
                             scalar=1.0 / D, in1=smalls[:, 24:28],
                             op0=ALU.mult, op1=ALU.subtract,
                             waits=[t_m2] + [S[f"sq_{c}"] for c in range(TC)])
                for c in range(TC):
                    S[f"var_{c}"] = t_var
                S["st_prev"] = t_var

        def emit_back_sd(tc):
            if tc != 0:
                return
            t_sd = K.op("scalar", "activation", out=smalls[:, 24:28],
                        in_=smalls[:, 12:16], func=AF.Sqrt,
                        bias=smalls[:, 20:21],
                        waits=[S["var_0"], t_eps, S.get("sd_prev")])
            t_ri = K.op("vector", "reciprocal", out=smalls[:, 16:20],
                        in_=smalls[:, 24:28], waits=[t_sd])
            for c in range(TC):
                S[f"ri_{c}"] = t_ri
            S["sd_prev"] = t_ri

        def emit_back_fin(tc):
            t_y = K.op("vector", "tensor_scalar", out=y[:, tc, :],
                       in0=y[:, tc, :], scalar1=smalls[:, 8 + tc:9 + tc],
                       scalar2=smalls[:, 16 + tc:17 + tc],
                       op0=ALU.subtract, op1=ALU.mult,
                       waits=[S[f"ri_{tc}"], S["qa_read"]])
            S[f"y_w_{tc}"] = t_y
            t_xd = emit_quant(y[:, tc, :], xd_tok[:, tc, :], [t_y],
                              extra_out_waits=[S["xd_tok_read"]])
            S[f"xd_w_{tc}"] = t_xd
            S["qa_read"] = t_xd

        def emit_wo_pair(pg, tcp, use_cache_wo):
            banks = {0: pWo, 1: pF}
            for hc in range(HC):
                pbuf = hc % 3
                wtok = load_piece(
                    "wo", hc, f"ld_wo{pbuf}", wo_b[:, pbuf], woC,
                    lambda hc=hc: woT[hc * P:(hc + 1) * P, :],
                    S["wo_read"][pbuf], use_cache_wo)
                for tci in range(2):
                    tc = tcp * 2 + tci
                    pbank = banks[tci]
                    w0 = [wtok, S[f"gh_w_{hc}"]]
                    if hc == 0:
                        w0.append(pWoR[0] if tci == 0 else pFR[0])
                    t1 = K.op("tensor", "matmul", pbank[:, 0:TT],
                              ghb[:, hc, tc * P:(tc + 1) * P],
                              wo_b[:, pbuf, 0:TT],
                              start=(hc == 0), stop=(hc == HC - 1),
                              waits=w0, inc=0)
                    t2 = K.op("tensor", "matmul", pbank[:, TT:D],
                              ghb[:, hc, tc * P:(tc + 1) * P],
                              wo_b[:, pbuf, TT:D],
                              start=(hc == 0), stop=(hc == HC - 1),
                              inc=1 if (tci == 1 or hc == HC - 1) else 0)
                    if hc == HC - 1:
                        if tci == 0:
                            pWoW[0] = t2
                        else:
                            pFW[0] = t2
                        S["wo_mm"][tc] = t2
                    if tci == 1:
                        S["wo_read"][pbuf] = t2
            for tci in range(2):
                tc = tcp * 2 + tci
                emit_back_s(pg, tc, banks[tci])

        def emit_tile(g):
            seq_i = (g * TT) // SEQ
            toff = (g * TT) % SEQ
            seq_start = (toff == 0)
            pg = g - 1

            xt_toks = []
            for dc in range(DC):
                sl = dc % 3
                stg = stg32[:, sl * TT:(sl + 1) * TT]
                w = [S.get("xTb_read"), S.get(f"stgslot_{sl}"),
                     S.get("stg_read")]
                lx = K.op("sync", "dma_start", out=stg,
                          in_=xt[seq_i, dc * P:(dc + 1) * P, toff:toff + TT],
                          waits=w, dma=True, sem=f"ld_stg{sl}")
                cx = K.op("scalar", "copy", out=xTb[:, dc, :],
                          in_=stg, waits=[lx] + w)
                S[f"stgslot_{sl}"] = cx
                if g == 0:
                    S["stg_read"] = cx
                xt_toks.append(cx)
            if seq_start:
                cm = K.op("vector", "memset", carry[:], 0.0,
                          waits=S["carry_tok"])
                tm = K.op("vector", "memset", tailb[:], 0.0,
                          waits=S["tail_tok"])
                S["carry_tok"] = [cm] * HC
                S["tail_tok"] = [tm] * HC

            use_cache = (g > 0) and not os.environ.get("KM_NO_CACHE")
            S["wg_pre"] = {}

            m_order = list(range(HC, GC)) + list(range(0, HC))
            conv_rot = 0
            for mi, m in enumerate(m_order):
                pbuf = mi % 3
                wtok = load_piece(
                    "wi", m, f"ld_wi{pbuf}", wi_b[:, pbuf], wiC,
                        lambda m=m: wiT.rearrange("(k p) h -> p k h", p=P)[:, :, m * P:(m + 1) * P],
                    S["wi_read"][pbuf], use_cache)
                bank = mi % 3
                mm_waits = [wtok, pbfR[bank]]
                if mi == 0:
                    mm_waits += xt_toks
                tmm = None
                for k in range(DC):
                    tmm = K.op("tensor", "matmul", pbf[bank][:],
                               wi_b[:, pbuf, k, :], xTb[:, k, :],
                               start=(k == 0), stop=(k == DC - 1),
                               waits=mm_waits if k == 0 else [],
                               inc=0 if k < DC - 1 else 1)
                pbfW[bank] = tmm
                S["wi_read"][pbuf] = tmm
                if mi == len(m_order) - 1:
                    S["xTb_read"] = tmm
                if m >= HC:
                    c = m - HC
                    cr = conv_rot % 2
                    conv_rot += 1
                    scr = convs[:, cr, :]
                    acc = convacc[:, 0, :]
                    wv = [S["conv_read"][cr]] if S["conv_read"][cr] else []
                    t_lead = K.op("vector", "tensor_copy", out=scr[:, 0:3],
                                  in_=tailb[:, c, :],
                                  waits=wv + [S["tail_tok"][c]])
                    t_scr = K.op("scalar", "copy", out=scr[:, 3:3 + TT],
                                 in_=pbf[bank][:], waits=[tmm] + wv)
                    wv2 = [S["convacc_read"][0]] if S["convacc_read"][0] else []
                    t_acc = K.op("scalar", "activation", out=acc,
                                 in_=pbf[bank][:], func=AF.Identity,
                                 scale=cw[:, 3 * HC + c:3 * HC + c + 1],
                                 bias=cb[:, c:c + 1],
                                 waits=[tmm, l_cw, l_cb] + wv2)
                    pbfR[bank] = t_acc
                    for kk in (2, 1, 0):
                        if kk != 0:
                            dst = acc
                            wv3 = [t_scr, t_acc, t_lead]
                        else:
                            dst = xh[:, c, :]
                            wv3 = [t_scr, t_acc, t_lead]
                            if S["xh_read"][c]:
                                wv3.extend(S["xh_read"][c])
                        t_acc = K.op("vector", "scalar_tensor_tensor",
                                     out=dst, in0=scr[:, kk:kk + TT],
                                     scalar=cw[:, kk * HC + c:kk * HC + c + 1],
                                     in1=acc, op0=ALU.mult, op1=ALU.add,
                                     waits=wv3)
                    S["convacc_read"][0] = t_acc
                    S[f"xh_w_{c}"] = t_acc
                    t_tail = K.op("vector", "tensor_copy", out=tailb[:, c, :],
                                  in_=scr[:, TT:TT + 3], waits=[t_scr])
                    S["tail_tok"][c] = t_tail
                    S["conv_read"][cr] = t_tail
                else:
                    c = m
                    wv = [S["gg_read"][c]] if S["gg_read"][c] else []
                    t_g = K.op("scalar", "activation", out=gg[:, c, :],
                               in_=pbf[bank][:], func=AF.Gelu,
                               waits=[tmm] + wv)
                    pbfR[bank] = t_g
                    S[f"gg_w_{c}"] = t_g
                if g > 0 and mi == 5:
                    S["s_toks"] = []
                    emit_wo_pair(pg, 0, pg > 0 and not os.environ.get("KM_NO_CACHE"))
                if g > 0 and mi == 11:
                    emit_wo_pair(pg, 1, not os.environ.get("KM_NO_CACHE"))
                    for gi in range(HC):
                        S["ghb_read"][gi] = K.milestone("tensor")
                    S["have_back"] = True
                    S["xtok_read"] = list(S["s_toks"])
                if g > 0 and mi in (13, 15, 17, 19):
                    emit_back_stats((mi - 13) // 2)
                if g == 0 and mi in (4, 6, 8, 10, 12, 14, 16, 18):
                    pump_fw()
                if 12 <= mi <= 14:
                    i6 = mi - 12
                    pgi, phalf = divmod(i6, 2)
                    m2 = pgi if phalf == 0 else HC + pgi
                    pb = i6 % 3
                    S["wg_pre"][(pgi, phalf)] = load_piece(
                        "wg", m2, f"ld_wg{pb}", wg_b[:, pb], wgC,
                        lambda m2=m2: wgT.rearrange("(k p) h -> p k h", p=P)[:, :, m2 * P:(m2 + 1) * P],
                        S["wg_read"][pb], use_cache)

            xh_all = [S[f"xh_w_{c}"] for c in range(HC)]
            l_xtok = K.op("scalar", "dma_start", out=xtok[:],
                          in_=xtok_d.rearrange("(n c p) d -> n p c d",
                                               c=TC, p=P)[g],
                          waits=S["xtok_read"], dma=True, sem="ld_xtok")
            S["l_xtok"] = l_xtok
            S["xtok_read"] = []

            has_prev = bool(S.get("have_back"))
            wg_ap = [pbf[0][:], pbf[1][:], pbf[2][:],
                     pWo[:, 0:TT], pWo[:, TT:2 * TT]]
            pbfR[3] = pbfR[4] = pWoR[0]
            rot = 0
            for gi in range(HC):
                if g == 0 and gi == 10:
                    emit_wd_transposes()
                grp_i = gi % GRP
                halves = []
                for half, m in ((0, gi), (1, HC + gi)):
                    pbuf = (2 * gi + half) % 3
                    if (gi, half) in S["wg_pre"]:
                        wtok = S["wg_pre"].pop((gi, half))
                    else:
                        wtok = load_piece(
                            "wg", m, f"ld_wg{pbuf}", wg_b[:, pbuf], wgC,
                            lambda m=m: wgT.rearrange("(k p) h -> p k h", p=P)[:, :, m * P:(m + 1) * P],
                            S["wg_read"][pbuf], use_cache)
                    bank = rot % 5
                    rot += 1
                    mm_waits = [wtok, pbfR[bank]]
                    if gi == 0 and half == 0:
                        mm_waits += xh_all
                    tmm = None
                    for k in range(HC):
                        tmm = K.op("tensor", "matmul", wg_ap[bank],
                                   wg_b[:, pbuf, k, :], xh[:, k, :],
                                   start=(k == 0), stop=(k == HC - 1),
                                   waits=mm_waits if k == 0 else [],
                                   inc=0 if k < HC - 1 else 1)
                    pbfW[bank] = tmm
                    S["wg_read"][pbuf] = tmm
                    halves.append((bank, tmm))
                bank_f, tok_f = halves[0]
                bank_i, tok_i = halves[1]
                ascr = convs[:, gi % 2, 0:TT]
                _ar = S["abc_read"][grp_i]
                wv = ([] if not _ar else (_ar if isinstance(_ar, list) else [_ar]))
                wva = [S[f"ascr_read_{gi % 2}"]] if S.get(f"ascr_read_{gi % 2}") else [S["conv_read"][gi % 2]]
                t_tf = K.op("scalar", "activation", out=bufA[:, grp_i, :],
                            in_=wg_ap[bank_f], func=AF.Tanh, scale=0.5,
                            bias=bgh[:, gi:gi + 1],
                            waits=[tok_f, t_bgh] + wv)
                pbfR[bank_f] = t_tf
                t_al = K.op("scalar", "activation", out=bufA[:, grp_i, :],
                            in_=bufA[:, grp_i, :], func=AF.Exp,
                            scale=chalf[:, gi:gi + 1],
                            bias=chalf[:, gi:gi + 1],
                            waits=[t_tf, t_ch])
                t_asq = K.op("scalar", "activation", out=ascr,
                             in_=bufA[:, grp_i, :], func=AF.Square,
                             waits=[t_al] + wva)
                t_bsq = K.op("vector", "tensor_scalar", out=bufC[:, grp_i, :],
                             in0=ascr, scalar1=-1.0,
                             scalar2=1.0 + EPS_B, op0=ALU.mult, op1=ALU.add,
                             waits=[t_asq] + wv)
                S[f"ascr_read_{gi % 2}"] = t_bsq
                t_ti = K.op("scalar", "activation", out=bufB[:, grp_i, :],
                            in_=wg_ap[bank_i], func=AF.Tanh, scale=0.5,
                            bias=bgh[:, HC + gi:HC + gi + 1],
                            waits=[tok_i, t_bgh] + wv)
                pbfR[bank_i] = t_ti
                t_xp = K.op("vector", "scalar_tensor_tensor",
                            out=bufB[:, grp_i, :], in0=bufB[:, grp_i, :],
                            scalar=1.0, in1=xh[:, gi, :],
                            op0=ALU.add, op1=ALU.mult,
                            waits=[t_ti, xh_all[gi]])
                S[f"chain_{gi}"] = (t_bsq, t_xp, t_al)
                if has_prev:
                    if gi == 6:
                        emit_back_fin(0)
                    if gi == 8:
                        emit_back_fin(1)
                if grp_i == GRP - 1:
                    emit_scan_group(g, gi - GRP + 1,
                                    with_sd=(has_prev and gi == GRP - 1))

            if has_prev:
                emit_ffn_prev(g)

            wg_done = K.milestone("tensor")
            for c in range(HC):
                S["xh_read"][c] = [S[f"gh_w_{c}"], wg_done]
            pwr = [t for t in (pbfR[3], pbfR[4]) if t]
            if pwr:
                pWoR[0] = max(pwr, key=lambda t: t[1])

            S["prev_g"] = g
            for j in (0, 1):
                if S.get(f"ascr_read_{j}"):
                    S["conv_read"][j] = S[f"ascr_read_{j}"]

        dbg = {}
        if debug:
            dbg["d_xh"] = nc.dram_tensor("d_xh", [P, HC, TT], BF16, kind="ExternalOutput").ap()
            dbg["d_gg"] = nc.dram_tensor("d_gg", [P, HC, TT], BF16, kind="ExternalOutput").ap()
            dbg["d_ghb"] = nc.dram_tensor("d_ghb", [P, HC, TT], BF16, kind="ExternalOutput").ap()
            dbg["d_y"] = nc.dram_tensor("d_y", [P, TC, D], F32, kind="ExternalOutput").ap()
            dbg["d_xd"] = nc.dram_tensor("d_xd", [P, TC, D], BF16, kind="ExternalOutput").ap()
            dbg["d_xdf"] = nc.dram_tensor("d_xdf", [P, DC, TT], BF16, kind="ExternalOutput").ap()
            dbg["d_wdT"] = nc.dram_tensor("d_wdT", [P, DC, D], BF16, kind="ExternalOutput").ap()
            dbg["d_woC"] = nc.dram_tensor("d_woC", [HC, P, D], BF16, kind="ExternalOutput").ap()
            dbg["d_wiC"] = nc.dram_tensor("d_wiC", [GC, P, DC * P], BF16, kind="ExternalOutput").ap()
            dbg["d_wgC"] = nc.dram_tensor("d_wgC", [GC, P, HC * P], BF16, kind="ExternalOutput").ap()
        def flush_back(n_tiles):
            S["s_toks"] = []
            emit_wo_pair(n_tiles - 1, 0,
                         n_tiles > 1 and not os.environ.get("KM_NO_CACHE"))
            emit_wo_pair(n_tiles - 1, 1, not os.environ.get("KM_NO_CACHE"))
            for tc in range(TC):
                emit_back_stats(tc)
            for tc in range(TC):
                emit_back_sd(tc)
            emit_back_fin(0)
            emit_back_fin(1)
            S["have_back"] = True
            emit_ffn_prev(n_tiles)

        for g in range(n_tiles):
            emit_tile(g)
            if debug and g == 0:
                for nm, sbuf in [("d_xh", xh), ("d_gg", gg), ("d_ghb", ghb),
                                 ("d_y", y), ("d_xd", xd_tok), ("d_wdT", wdT)]:
                    K.op("scalar", "dma_start", out=dbg[nm], in_=sbuf[:],
                         waits=[K.milestone("vector"), K.milestone("scalar")],
                         dma=True, sem="st_dbg")
        flush_back(n_tiles)
        if debug:
            K.op("scalar", "dma_start", out=dbg["d_xdf"], in_=xd_feat[:],
                 waits=[K.milestone("scalar"), K.milestone("tensor")], dma=True,
                 sem="st_dbg")
            K.op("scalar", "dma_start", out=dbg["d_woC"], in_=woC,
                 waits=[K.milestone("scalar")], dma=True, sem="st_dbg")
            K.op("scalar", "dma_start", out=dbg["d_wiC"], in_=wiC,
                 waits=[K.fence("st_wi")], dma=True, sem="st_dbg")
            K.op("scalar", "dma_start", out=dbg["d_wgC"], in_=wgC,
                 waits=[K.fence("st_wg")], dma=True, sem="st_dbg")

        K.wait("sync", [K.milestone(e) for e in ENGINES]
               + [K.fence(ch) for ch in list(K.cval.keys())])

        K.build(nc, stack, simsafe=bool(os.environ.get("BASS_SIM_FP64"))
                or bool(os.environ.get("KM_SIMSAFE")))
    return nc


_CACHE = {}


def _host_prep(L0_attn_x, L0_attn_Wi, L0_attn_conv_w, L0_attn_conv_b,
               L0_attn_Wg, L0_attn_bg, L0_attn_forget_base, L0_attn_Wo,
               L0_ffn_weight, L0_ffn_bias):
    x = np.ascontiguousarray(np.asarray(L0_attn_x, dtype=np.float32))
    Wi = np.ascontiguousarray(np.asarray(L0_attn_Wi, np.float32))
    Wg = np.ascontiguousarray(np.asarray(L0_attn_Wg, np.float32))
    Wo = np.ascontiguousarray(np.asarray(L0_attn_Wo, np.float32))
    cwv = np.asarray(L0_attn_conv_w, np.float32)
    cbv = np.asarray(L0_attn_conv_b, np.float32)
    fbv = np.asarray(L0_attn_forget_base, np.float32)
    bgv = np.asarray(L0_attn_bg, np.float32)
    fw = np.ascontiguousarray(np.asarray(L0_ffn_weight, np.float32))
    fbias = np.ascontiguousarray(np.asarray(L0_ffn_bias, np.float32).reshape(1, D))

    wiT = np.ascontiguousarray(Wi.T)
    wgT = np.ascontiguousarray(Wg.T)
    woT = np.ascontiguousarray(Wo.T)
    cw_h = np.ascontiguousarray(
        cwv.reshape(HC, P, 4).transpose(1, 2, 0).reshape(P, 4 * HC))
    cb_h = np.ascontiguousarray(cbv.reshape(HC, P).T)
    fb_h = np.ascontiguousarray(fbv.reshape(HC, P).T)
    bg_h = np.ascontiguousarray(bgv.reshape(2 * HC, P).T)
    identb = np.eye(P, dtype=ml_dtypes.bfloat16)

    shared = dict(wiT=wiT, wgT=wgT, woT=woT, cw=cw_h, cb=cb_h, fb=fb_h,
                  bg=bg_h, fw=fw, fbias=fbias, identb=identb)
    in_maps = []
    for core in range(8):
        xc = x[2 * core:2 * core + 2]
        xt = np.ascontiguousarray(xc.transpose(0, 2, 1))
        xtok = np.ascontiguousarray(xc.reshape(NSEQ * SEQ, D))
        m = dict(shared)
        m["xt"] = xt
        m["xtok"] = xtok
        in_maps.append(m)
    return in_maps


def kernel(**inputs):
    if "nc" not in _CACHE:
        _CACHE["nc"] = build_nc()
    nc = _CACHE["nc"]
    in_maps = _host_prep(**inputs)
    res = run_bass_kernel_spmd(nc, in_maps, core_ids=list(range(8)))
    outs = [np.asarray(r["out"]).reshape(NSEQ, SEQ, D) for r in res.results]
    return np.concatenate(outs, axis=0)



# version 17 (final)
# speedup vs baseline: 1.0133x; 1.0133x over previous
import contextlib
import os
import numpy as np
import ml_dtypes
import concourse.bass as bass
import concourse.mybir as mybir
from concourse.bass_utils import run_bass_kernel_spmd

F32 = mybir.dt.float32
BF16 = mybir.dt.bfloat16
I32 = mybir.dt.int32
AF = mybir.ActivationFunctionType
ALU = mybir.AluOpType
AX = mybir.AxisListType

P = 128
D = 1024
H = 1536
G = 3072
TT = 512
SEQ = 2048
NSEQ = 2
N_TILES = (SEQ // TT) * NSEQ
TC = TT // P
DC = D // P
HC = H // P
GC = G // P
GRP = 6
EPS_LN = 1e-5
EPS_B = 1e-6

ENGINES = ["sync", "scalar", "vector", "tensor", "gpsimd"]


class Prog:

    RISKY = ("tensor_reduce", "reciprocal", "tensor_tensor_scan", "bn_stats")

    def __init__(self):
        self.ops = {e: [] for e in ENGINES}
        self.val = {e: 0 for e in ENGINES}
        self.cval = {}
        self.waited = {c: {} for c in ENGINES}
        self.risky = set()

    def op(self, eng, method, *args, waits=(), dma=False, sem=None,
           inc=None, **kwargs):
        if dma:
            assert sem is not None, "dma ops need a channel sem"
            self.cval[sem] = self.cval.get(sem, 0) + 16
            key, val_after = sem, self.cval[sem]
            inc = 16
        else:
            assert sem is None
            if inc is None:
                inc = 1
            self.val[eng] += inc
            key, val_after = eng, self.val[eng]
        wl = {}
        for tok in waits:
            if tok is None:
                continue
            wkey, wval = tok
            wl[wkey] = max(wl.get(wkey, 0), wval)
        fw = []
        for wkey, wval in wl.items():
            if wkey == eng:
                if wval > self.val[eng]:
                    raise RuntimeError(f"self-wait in future: {eng} {wval}")
                fw.append((wkey, wval))
            elif wval > self.waited[eng].get(wkey, 0):
                fw.append((wkey, wval))
                self.waited[eng][wkey] = wval
        self.ops[eng].append((method, args, kwargs, fw, inc if not dma else 16,
                              key if (dma or inc) else None))
        if method in self.RISKY or kwargs.get("accum_out") is not None:
            self.risky.add((key, val_after))
        return (key, val_after)

    def wait(self, eng, waits):
        return self.op(eng, "__wait__", waits=waits, inc=0)

    def milestone(self, eng):
        return (eng, self.val[eng])

    def fence(self, ch):
        return (ch, self.cval.get(ch, 0))

    def build(self, nc, stack, simsafe=False):
        ops2 = {}
        for e in ENGINES:
            lst = []
            riskydone = 0
            for method, args, kwargs, waits, inc, key in self.ops[e]:
                if not simsafe:
                    keepall = e in os.environ.get("KM_SELF", "vector")
                    w2 = []
                    for wk, wv in waits:
                        if wk != e:
                            w2.append((wk, wv))
                        elif keepall or ((e, wv) in self.risky and wv > riskydone):
                            w2.append((wk, wv))
                            riskydone = max(riskydone, wv)
                    waits = w2
                    if method == "__wait__" and not waits:
                        continue
                lst.append([method, args, kwargs, waits, inc, key])
            ops2[e] = lst
        if not simsafe:
            referenced = {e: set() for e in ENGINES}
            for e in ENGINES:
                for op in ops2[e]:
                    for wk, wv in op[3]:
                        if wk in referenced:
                            referenced[wk].add(wv)
            remap = {}
            for e in ENGINES:
                m = {}
                orig = 0
                new = 0
                for op in ops2[e]:
                    method, args, kwargs, waits, inc, key = op
                    if inc and key == e:
                        orig += inc
                        if orig in referenced[e]:
                            new += 1
                            op[4] = 1
                        else:
                            op[4] = 0
                        m[orig] = new
                remap[e] = m
            for e in ENGINES:
                for op in ops2[e]:
                    op[3] = [(wk, remap[wk][wv] if wk in remap else wv)
                             for wk, wv in op[3]]
        keys = list(ENGINES) + sorted(self.cval.keys())
        sems = {k: stack.enter_context(nc.semaphore(f"s_{k}")) for k in keys}
        block = stack.enter_context(nc.Block())
        for e in ENGINES:
            ops = ops2[e]
            if not ops:
                continue

            def make_runner(eng_name, eng_ops):
                def runner(engine):
                    for method, args, kwargs, waits, inc, key in eng_ops:
                        for wkey, wval in waits:
                            engine.wait_ge(sems[wkey], wval)
                        if method == "__wait__":
                            continue
                        inst = getattr(engine, method)(*args, **kwargs)
                        if inc and key is not None:
                            inst.then_inc(sems[key], inc)
                return runner

            getattr(block, e)(make_runner(e, ops))


def build_nc(n_tiles=N_TILES, debug=False):
    nc = bass.Bass("TRN2", target_bir_lowering=False)

    def dI(name, shape, dt=F32):
        return nc.dram_tensor(name, shape, dt, kind="ExternalInput").ap()

    xt = dI("xt", [NSEQ, D, SEQ])
    xtok_d = dI("xtok", [NSEQ * SEQ, D])
    wiT = dI("wiT", [D, G])
    wgT = dI("wgT", [H, G])
    woT = dI("woT", [H, D])
    cw_d = dI("cw", [P, 4 * HC])
    cb_d = dI("cb", [P, HC])
    fb_d = dI("fb", [P, HC])
    bg_d = dI("bg", [P, 2 * HC])
    fw_d = dI("fw", [D, D])
    fbias_d = dI("fbias", [1, D])
    identb_d = dI("identb", [P, P], BF16)
    out_d = nc.dram_tensor("out", [NSEQ * SEQ, D], F32, kind="ExternalOutput").ap()

    wiC = nc.dram_tensor("wiC", [GC, P, DC * P], BF16, kind="Internal").ap()
    wdC = nc.dram_tensor("wdC", [DC, P, D], BF16, kind="Internal").ap()
    wgC = nc.dram_tensor("wgC", [GC, P, HC * P], BF16, kind="Internal").ap()
    woC = nc.dram_tensor("woC", [HC, P, D], BF16, kind="Internal").ap()

    stack = contextlib.ExitStack()

    def sb(name, shape, dt=F32):
        return stack.enter_context(nc.sbuf_tensor(name, shape, dt))

    def psum(name, shape, dt=F32):
        return stack.enter_context(nc.psum_tensor(name, shape, dt))

    with stack:
        xTb = sb("xTb", [P, DC, TT], BF16)
        xtok = sb("xtok_s", [P, TC, D])
        NPB = 3
        wi_b = sb("wi_b", [P, NPB, DC, P], BF16)
        wg_b = sb("wg_b", [P, NPB, HC, P], BF16)
        wo_b = sb("wo_b", [P, NPB, D], BF16)
        stg32 = sb("stg32", [P, HC * P])
        gg = sb("gg", [P, HC, TT], BF16)
        xh = sb("xh", [P, HC, TT], BF16)
        ghb = sb("ghb", [P, HC, TT], BF16)
        bufA = sb("bufA", [P, GRP, TT])
        bufB = sb("bufB", [P, GRP, TT])
        bufC = sb("bufC", [P, GRP, TT])
        convs = sb("convs", [P, 2, 3 + TT])
        convacc = sb("convacc", [P, 1, TT])
        tailb = sb("tailb", [P, HC, 3])
        carry = sb("carry", [P, HC])
        cw = sb("cw_s", [P, 4 * HC])
        cb = sb("cb_s", [P, HC])
        fbs = sb("fbs_s", [P, HC])
        chalf = sb("chalf_s", [P, HC])
        bgh = sb("bgh_s", [P, 2 * HC])
        fbias = sb("fbias_s", [1, D])
        identb = sb("identb_s", [P, P], BF16)
        ones1 = sb("ones1", [1, P])
        y = sb("y", [P, TC, D])
        qa = sb("qa", [P, D])
        qb = sb("qb", [P, D])
        qd = sb("qd", [P, D])
        qi = sb("qi", [P, D], I32)
        qf = qi
        m8 = sb("m8", [P, 32])
        scl = sb("scl", [P, 32])
        invs = sb("invs", [P, 32])
        xd_tok = sb("xd_tok", [P, TC, D], BF16)
        xd_feat = sb("xd_feat", [P, DC, TT], BF16)
        wdT = sb("wdT", [P, DC, D], BF16)
        wd_p = sb("wd_p", [P, D], BF16)
        wdcol = sb("wdcol", [P, DC, P], BF16)
        outstg = sb("outstg", [P, 1, D])
        qc = outstg[:, 0, :]
        smalls = sb("smalls", [P, 32])
        det = sb("det", [1, 4])
        deti = sb("deti", [1, 1], I32)
        offp = sb("offp", [P, 1])

        pbf = [psum(f"pbf{i}", [P, TT]) for i in range(3)]
        pWo = psum("pWo", [P, D])
        pF = psum("pF", [P, D])
        pT = psum("pT", [P, D], BF16)

        K = Prog()
        S = {}
        S["outstg_read"] = [None]

        pbfW = [None, None, None, None, None]
        pbfR = [None, None, None, None, None]
        pWoW = [None]
        pWoR = [None]
        pFW = [None]
        pFR = [None]
        pTW = [None]
        pTR = [None]

        for _dst, _src in [(cw[:], cw_d), (cb[:], cb_d), (fbs[:], fb_d),
                           (bgh[:], bg_d), (fbias[:], fbias_d),
                           (identb[:], identb_d)]:
            K.op("sync", "dma_start", out=_dst, in_=_src, dma=True,
                 sem="ld_misc")
        misc_f = K.fence("ld_misc")
        l_cw = l_cb = l_fb = l_bg = l_fbias = l_identb = misc_f

        t_abs = K.op("scalar", "activation", out=smalls[:, 0:12], in_=fbs[:],
                     func=AF.Abs, waits=[l_fb])
        t_e = K.op("scalar", "activation", out=smalls[:, 0:12],
                   in_=smalls[:, 0:12], func=AF.Exp, scale=-1.0, waits=[t_abs])
        t_l = K.op("scalar", "activation", out=smalls[:, 0:12],
                   in_=smalls[:, 0:12], func=AF.Ln, bias=1.0, waits=[t_e])
        t_r = K.op("vector", "tensor_scalar", out=chalf[:], in0=fbs[:],
                   scalar1=0.0, scalar2=None, op0=ALU.max, waits=[l_fb])
        t_sp = K.op("vector", "tensor_tensor", out=chalf[:], in0=chalf[:],
                    in1=smalls[:, 0:12], op=ALU.add, waits=[t_l, t_r])
        t_ch = K.op("vector", "tensor_scalar", out=chalf[:], in0=chalf[:],
                    scalar1=-4.0, scalar2=None, op0=ALU.mult, waits=[t_sp])
        t_bgh = K.op("vector", "tensor_scalar", out=bgh[:], in0=bgh[:],
                     scalar1=0.5, scalar2=None, op0=ALU.mult, waits=[l_bg])
        t_ones = K.op("vector", "memset", ones1[:], 1.0)
        t_eps = K.op("vector", "memset", smalls[:, 20:21], EPS_LN)
        t_d0 = K.op("vector", "memset", det[:, 0:1], 67.5)
        t_d1 = K.op("vector", "tensor_scalar", out=deti[:], in0=det[:, 0:1],
                    scalar1=1.0, scalar2=None, op0=ALU.mult, waits=[t_d0])
        t_d2 = K.op("vector", "tensor_scalar", out=det[:, 1:2], in0=deti[:],
                    scalar1=-0.5, scalar2=98.0, op0=ALU.mult, op1=ALU.add,
                    waits=[t_d1])
        t_off_mm = K.op("tensor", "matmul", pbf[0][:, 0:1], ones1[:],
                        det[:, 1:2], start=True, stop=True,
                        waits=[t_ones, t_d2])
        pbfW[0] = t_off_mm
        t_off = K.op("vector", "tensor_copy", out=offp[:], in_=pbf[0][:, 0:1],
                     waits=[t_off_mm])
        pbfR[0] = t_off

        def emit_quant(y_ap, out_bf_ap, pre_waits, extra_out_waits=()):
            w = [t for t in pre_waits if t is not None]
            if S.get("quant_prev"):
                w.append(S["quant_prev"])
            t_m8 = K.op("vector", "tensor_reduce", out=m8[:],
                        in_=y_ap.rearrange("p (b k) -> p b k", k=32),
                        axis=AX.X, op=ALU.max, apply_absolute_value=True,
                        waits=w)
            t_sc = K.op("vector", "tensor_scalar", out=scl[:], in0=m8[:],
                        scalar1=1.0 / 7.5, scalar2=1e-30, op0=ALU.mult,
                        op1=ALU.max, waits=[t_m8])
            t_iv = K.op("vector", "reciprocal", out=invs[:], in_=scl[:],
                        waits=[t_sc])
            ivb = invs[:].unsqueeze(2).broadcast_to([P, 32, 32])
            t_u8 = K.op("vector", "scalar_tensor_tensor",
                        out=qa[:].rearrange("p (b k) -> p b k", k=32),
                        in0=y_ap.rearrange("p (b k) -> p b k", k=32),
                        scalar=8.0, in1=ivb, op0=ALU.mult, op1=ALU.mult,
                        waits=[t_iv])
            t_eb = K.op("vector", "tensor_scalar", out=qi[:],
                        in0=qa[:].bitcast(I32), scalar1=0x7F800000,
                        scalar2=None, op0=ALU.bitwise_and, waits=[t_u8])
            ebf = qi[:].bitcast(F32)
            t_c1 = K.op("vector", "tensor_scalar", out=qb[:], in0=ebf,
                        scalar1=16.0, scalar2=-0.5, op0=ALU.is_ge,
                        op1=ALU.mult, waits=[t_eb])
            t_c2 = K.op("vector", "tensor_scalar", out=qc, in0=ebf,
                        scalar1=32.0, scalar2=-0.25, op0=ALU.is_ge,
                        op1=ALU.mult,
                        waits=[t_eb, S["outstg_read"][0] if S.get("outstg_read") else None])
            t_ig = K.op("vector", "scalar_tensor_tensor", out=qd[:], in0=qb[:],
                        scalar=1.0, in1=qc, op0=ALU.add, op1=ALU.add,
                        waits=[t_c1, t_c2])
            t_w = K.op("vector", "tensor_tensor", out=qa[:], in0=qa[:],
                       in1=qd[:], op=ALU.mult, waits=[t_ig])
            t_fi = K.op("vector", "tensor_scalar", out=qi[:], in0=qa[:],
                        scalar1=offp[:], scalar2=None, op0=ALU.add,
                        waits=[t_w, t_off])
            t_z2 = K.op("vector", "tensor_scalar", out=qd[:], in0=qc,
                        scalar1=-1.0, scalar2=0.125, op0=ALU.mult, op1=ALU.add,
                        waits=[t_ig, t_w])
            t_st = K.op("vector", "scalar_tensor_tensor", out=qb[:], in0=qb[:],
                        scalar=-0.25, in1=qd[:], op0=ALU.mult, op1=ALU.add,
                        waits=[t_z2])
            sclb = scl[:].unsqueeze(2).broadcast_to([P, 32, 32])
            t_ss = K.op("vector", "tensor_tensor",
                        out=qd[:].rearrange("p (b k) -> p b k", k=32),
                        in0=qb[:].rearrange("p (b k) -> p b k", k=32),
                        in1=sclb, op=ALU.mult, waits=[t_st])
            t_xd = K.op("vector", "scalar_tensor_tensor", out=out_bf_ap,
                        in0=qi[:], scalar=-64.0, in1=qd[:],
                        op0=ALU.add, op1=ALU.mult,
                        waits=[t_fi, t_ss] + [t for t in extra_out_waits if t])
            S["quant_prev"] = t_xd
            return t_xd

        S["fw_jc"] = 0
        S["fw_slot_read"] = [None, None]
        S["fw_dma_prev"] = None

        def pump_fw():
            jc = S["fw_jc"]
            if jc >= DC:
                return
            S["fw_jc"] = jc + 1
            sl = jc % 2
            w = [S["fw_slot_read"][sl]] if S["fw_slot_read"][sl] else []
            l = K.op("sync", "dma_start", out=y[:, sl, :],
                     in_=fw_d.rearrange("(c p) d -> c p d", p=P)[jc],
                     waits=w, dma=True, sem="ld_fw")
            t_q = emit_quant(y[:, sl, :], wd_p[:],
                             [l] + ([S["fw_dma_prev"]] if S["fw_dma_prev"] else []))
            S["fw_slot_read"][sl] = t_q
            S["fw_dma_prev"] = K.op("scalar", "dma_start", out=wdC[jc],
                                    in_=wd_p[:], waits=[t_q], dma=True,
                                    sem="st_wd")

        def emit_wd_transposes():
            t_evac = None
            t_col = None
            t_col_done = None
            wd_fence = K.fence("st_wd")
            for dc in range(DC):
                w = [wd_fence]
                if t_col is not None:
                    w.append(t_col_done)
                t_col = K.op("sync", "dma_start", out=wdcol[:],
                             in_=wdC.rearrange("j p d -> p j d")[:, :, dc * P:(dc + 1) * P],
                             waits=w, dma=True, sem="ld_wdcol")
                tr_last = None
                for jc in range(DC):
                    w2 = [t_col, l_identb]
                    if pTR[0] is not None:
                        w2.append(pTR[0])
                    tr_last = K.op("tensor", "transpose",
                                   pT[:, jc * P:(jc + 1) * P],
                                   wdcol[:, jc, :],
                                   identb[:], waits=w2)
                pTW[0] = tr_last
                t_col_done = tr_last
                t_evac = K.op("scalar", "copy", out=wdT[:, dc, :], in_=pT[:],
                              waits=[tr_last])
                pTR[0] = t_evac
            S["t_wdT"] = t_evac
            S["y_read"] = [S["fw_slot_read"][0], S["fw_slot_read"][1],
                           None, None]

        S["xTb_read"] = None
        S["xtstg_read"] = None
        S["xtok_read"] = []
        S["wi_read"] = [None, None, None]
        S["wg_read"] = [None, None, None]
        S["wo_read"] = [None, None, None]
        S["wi_cached"] = [None] * GC
        S["wg_cached"] = [None] * GC
        S["wo_cached"] = [None] * HC
        S["gg_read"] = [None] * HC
        S["xh_read"] = [None] * HC
        S["conv_read"] = [None, None]
        S["convacc_read"] = [None, None]
        S["abc_read"] = [None] * GRP
        S["carry_tok"] = [None] * HC
        S["tail_tok"] = [None] * HC
        S["quant_toks"] = None
        S["y_read"] = [None] * TC
        S["s_toks"] = []
        S["ghb_read"] = [None] * HC
        S["xd_tok_read"] = None
        S["qa_read"] = None
        S["out_toks"] = []
        S["prev_g"] = None
        S["xd_feat_read"] = None
        S["xd_feat_w"] = [None] * TC
        S["wo_mm"] = [None] * TC
        S["l_xtok"] = None

        def load_piece(kind, idx, ch, dst_ap, cacheC, src_fn, read_tok,
                       use_cache):
            w = [read_tok] if read_tok else []
            if dst_ap.ndim == 3:
                cols = dst_ap.shape[-1] * dst_ap.shape[-2]
                dst2 = dst_ap.rearrange("p a b -> p (a b)")
            else:
                cols = dst_ap.shape[-1]
                dst2 = dst_ap
            if not use_cache:
                w.append(K.fence(f"st_{kind}"))
                lw = [S["stg_read"]] if S["stg_read"] else []
                stg = stg32[:, 0:cols]
                stg_in = (stg.rearrange("p (a b) -> p a b", b=dst_ap.shape[-1])
                          if dst_ap.ndim == 3 else stg)
                l = K.op("scalar" if kind == "wo" else "sync", "dma_start",
                         out=stg_in, in_=src_fn(),
                         waits=w + lw, dma=True, sem="ld_stg")
                c = K.op("vector", "tensor_copy", out=dst2, in_=stg,
                         waits=[l] + w)
                S["stg_read"] = c
                st = K.op("scalar", "dma_start", out=cacheC[idx], in_=dst2,
                          waits=[c], dma=True, sem=f"st_{kind}")
                S[f"{kind}_cached"][idx] = st
                return c
            else:
                w.append(K.fence(f"st_{kind}"))
                return K.op("sync", "dma_start", out=dst2, in_=cacheC[idx],
                            waits=w, dma=True, sem=ch)

        def emit_ffn_prev(g):
            pg = g - 1
            out_toks = []

            def emit_xdt(tc):
                tr_last = None
                for dc in range(DC):
                    w = [S[f"xd_w_{tc}"], l_identb]
                    if dc == 0 and pTR[0] is not None:
                        w.append(pTR[0])
                    tr_last = K.op("tensor", "transpose",
                                   pT[:, dc * P:(dc + 1) * P],
                                   xd_tok[:, tc, dc * P:(dc + 1) * P],
                                   identb[:], waits=w)
                pTW[0] = tr_last
                wv = [S["xd_feat_read"]] if (tc == 0 and S["xd_feat_read"]) else []
                t_ev = K.op("scalar", "copy",
                            out=xd_feat[:, 0:DC, tc * P:(tc + 1) * P],
                            in_=pT[:].rearrange("p (a b) -> p a b", a=DC),
                            waits=[tr_last] + wv)
                pTR[0] = t_ev
                S["xd_feat_w"][tc] = t_ev

            emit_xdt(0)
            if S.get("fin2_done"):
                S["fin2_done"] = False
            else:
                emit_back_fin(2)
            emit_xdt(1)
            for tc in range(TC):
                if tc == 1:
                    emit_back_fin(3)
                if tc >= 2:
                    emit_xdt(tc)
                mm = None
                for dc in range(DC):
                    w = [S["xd_feat_w"][tc], S["t_wdT"]]
                    if dc == 0 and pFR[0] is not None:
                        w.append(pFR[0])
                    mm = K.op("tensor", "matmul", pF[:, 0:TT],
                              xd_feat[:, dc, tc * P:(tc + 1) * P],
                              wdT[:, dc, 0:TT],
                              start=(dc == 0), stop=False,
                              waits=w if dc == 0 else [], inc=0)
                    mm = K.op("tensor", "matmul", pF[:, TT:D],
                              xd_feat[:, dc, tc * P:(tc + 1) * P],
                              wdT[:, dc, TT:D],
                              start=(dc == 0), stop=False, inc=0)
                K.op("tensor", "matmul", pF[:, 0:TT], ones1[:],
                     fbias[:, 0:TT], start=False, stop=True,
                     waits=[l_fbias], inc=0)
                t_mm = K.op("tensor", "matmul", pF[:, TT:D], ones1[:],
                            fbias[:, TT:D], start=False, stop=True)
                pFW[0] = t_mm
                ob = 0
                wv = [S["outstg_read"][ob]] if S["outstg_read"][ob] else []
                t_v = K.op("vector", "scalar_tensor_tensor",
                           out=outstg[:, ob, :], in0=pF[:], scalar=1.0,
                           in1=y[:, tc, :], op0=ALU.mult, op1=ALU.add,
                           accum_out=smalls[:, 21:22],
                           waits=[t_mm, S[f"y_w_{tc}"], S.get("fp_prev"),
                                  S.get("quant_prev")] + wv)
                pFR[0] = t_v
                S["y_read"][tc] = t_v
                wq = [S["qa_read"]] if S["qa_read"] else []
                t_sq = K.op("scalar", "activation", out=qa[:],
                            in_=outstg[:, ob, :], func=AF.Square,
                            accum_out=smalls[:, 22:23], waits=[t_v] + wq)
                S["qa_read"] = t_sq
                t_m = K.op("vector", "tensor_scalar", out=smalls[:, 23:24],
                           in0=smalls[:, 21:22], scalar1=1.0 / D, scalar2=None,
                           op0=ALU.mult, waits=[t_v])
                t_m2 = K.op("vector", "tensor_tensor", out=smalls[:, 25:26],
                            in0=smalls[:, 23:24], in1=smalls[:, 23:24],
                            op=ALU.mult, waits=[t_m])
                t_var = K.op("vector", "scalar_tensor_tensor",
                             out=smalls[:, 26:27], in0=smalls[:, 22:23],
                             scalar=1.0 / D, in1=smalls[:, 25:26],
                             op0=ALU.mult, op1=ALU.subtract,
                             waits=[t_sq, t_m2])
                t_sd = K.op("scalar", "activation", out=smalls[:, 27:28],
                            in_=smalls[:, 26:27], func=AF.Sqrt,
                            bias=smalls[:, 20:21], waits=[t_var, t_eps])
                t_ri = K.op("vector", "reciprocal", out=smalls[:, 28:29],
                            in_=smalls[:, 27:28], waits=[t_sd])
                t_o = K.op("vector", "tensor_scalar", out=outstg[:, ob, :],
                           in0=outstg[:, ob, :], scalar1=smalls[:, 23:24],
                           scalar2=smalls[:, 28:29], op0=ALU.subtract,
                           op1=ALU.mult, waits=[t_ri, t_sq])
                t_dma = K.op("scalar", "dma_start",
                             out=out_d.rearrange("(n c p) d -> n c p d",
                                                 c=TC, p=P)[pg, tc],
                             in_=outstg[:, ob, :], waits=[t_o], dma=True,
                             sem="st_out")
                S["outstg_read"][ob] = t_dma
                S["fp_prev"] = t_o
                out_toks.append(t_dma)
            S["out_toks"] += out_toks
            S["have_back"] = False
            S["xd_tok_read"] = pTW[0]
            S["xd_feat_read"] = K.milestone("tensor")

        def emit_scan_group(g, g0, with_sd=False):
            for j in range(GRP):
                gi = g0 + j
                t_b = K.op("scalar", "activation", out=bufC[:, j, :],
                           in_=bufC[:, j, :], func=AF.Sqrt,
                           waits=[S[f"chain_{gi}"][0]])
                t_bsq, t_xp, t_al = S[f"chain_{gi}"]
                t_xg = K.op("vector", "scalar_tensor_tensor",
                            out=bufB[:, j, :], in0=bufC[:, j, :], scalar=0.5,
                            in1=bufB[:, j, :], op0=ALU.mult, op1=ALU.mult,
                            waits=[t_b, t_xp])
                t_h = K.op("vector", "tensor_tensor_scan", out=bufC[:, j, :],
                           data0=bufA[:, j, :], data1=bufB[:, j, :],
                           initial=carry[:, gi:gi + 1],
                           op0=ALU.mult, op1=ALU.add,
                           waits=[t_xg, t_al, S["carry_tok"][gi]])
                S[f"h_{gi}"] = t_h
                wv = [S["ghb_read"][gi]] if S["ghb_read"][gi] else []
                t_gh = K.op("vector", "tensor_tensor", out=ghb[:, gi, :],
                            in0=gg[:, gi, :], in1=bufC[:, j, :], op=ALU.mult,
                            waits=[t_h, S[f"gg_w_{gi}"]] + wv)
                S[f"gh_w_{gi}"] = t_gh
                S["abc_read"][j] = t_gh
                S["gg_read"][gi] = t_gh
            t_c = K.op("vector", "tensor_copy", out=carry[:, g0:g0 + GRP],
                       in_=bufC[:, :, TT - 1:TT].rearrange("p a b -> p (a b)"),
                       waits=[S[f"h_{g0 + j}"] for j in range(GRP)])
            for j in range(GRP):
                S["carry_tok"][g0 + j] = t_c
                S["abc_read"][j] = [S["abc_read"][j], t_c]
            if with_sd:
                for tc in range(TC):
                    emit_back_sd(tc)

        def emit_back_s(g, tc, pbank):
            t_s = K.op("vector", "scalar_tensor_tensor", out=y[:, tc, :],
                       in0=pbank[:], scalar=1.0, in1=xtok[:, tc, :],
                       op0=ALU.mult, op1=ALU.add,
                       accum_out=smalls[:, tc:tc + 1],
                       waits=[S["wo_mm"][tc], S["l_xtok"],
                              S["y_read"][tc]])
            if pbank is pWo:
                pWoR[0] = t_s
            else:
                pFR[0] = t_s
            S["s_toks"].append(t_s)
            S[f"s_done_{tc}"] = t_s

        def emit_back_stats(tc):
            wq = [S["qa_read"]] if S["qa_read"] else []
            t_s = S[f"s_done_{tc}"]
            t_sq = K.op("scalar", "activation", out=qa[:], in_=y[:, tc, :],
                        func=AF.Square, accum_out=smalls[:, 4 + tc:5 + tc],
                        waits=[t_s] + wq
                        + ([S["st_prev"]] if S.get("st_prev") else []))
            S["qa_read"] = t_sq
            S[f"sq_{tc}"] = t_sq
            if tc == TC - 1:
                t_m = K.op("vector", "tensor_scalar", out=smalls[:, 8:12],
                           in0=smalls[:, 0:4], scalar1=1.0 / D, scalar2=None,
                           op0=ALU.mult,
                           waits=[S[f"s_done_{c}"] for c in range(TC)]
                           + [S.get("st_prev")])
                t_m2 = K.op("vector", "tensor_tensor", out=smalls[:, 24:28],
                            in0=smalls[:, 8:12], in1=smalls[:, 8:12],
                            op=ALU.mult, waits=[t_m])
                t_var = K.op("vector", "scalar_tensor_tensor",
                             out=smalls[:, 12:16], in0=smalls[:, 4:8],
                             scalar=1.0 / D, in1=smalls[:, 24:28],
                             op0=ALU.mult, op1=ALU.subtract,
                             waits=[t_m2] + [S[f"sq_{c}"] for c in range(TC)])
                for c in range(TC):
                    S[f"var_{c}"] = t_var
                S["st_prev"] = t_var

        def emit_back_sd(tc):
            if tc != 0:
                return
            t_sd = K.op("scalar", "activation", out=smalls[:, 24:28],
                        in_=smalls[:, 12:16], func=AF.Sqrt,
                        bias=smalls[:, 20:21],
                        waits=[S["var_0"], t_eps, S.get("sd_prev")])
            t_ri = K.op("vector", "reciprocal", out=smalls[:, 16:20],
                        in_=smalls[:, 24:28], waits=[t_sd])
            for c in range(TC):
                S[f"ri_{c}"] = t_ri
            S["sd_prev"] = t_ri

        def emit_back_fin(tc):
            t_y = K.op("vector", "tensor_scalar", out=y[:, tc, :],
                       in0=y[:, tc, :], scalar1=smalls[:, 8 + tc:9 + tc],
                       scalar2=smalls[:, 16 + tc:17 + tc],
                       op0=ALU.subtract, op1=ALU.mult,
                       waits=[S[f"ri_{tc}"], S["qa_read"]])
            S[f"y_w_{tc}"] = t_y
            t_xd = emit_quant(y[:, tc, :], xd_tok[:, tc, :], [t_y],
                              extra_out_waits=[S["xd_tok_read"]])
            S[f"xd_w_{tc}"] = t_xd
            S["qa_read"] = t_xd

        def emit_wo_pair(pg, tcp, use_cache_wo):
            banks = {0: pWo, 1: pF}
            for hc in range(HC):
                pbuf = hc % 3
                wtok = load_piece(
                    "wo", hc, f"ld_wo{pbuf}", wo_b[:, pbuf], woC,
                    lambda hc=hc: woT[hc * P:(hc + 1) * P, :],
                    S["wo_read"][pbuf], use_cache_wo)
                for tci in range(2):
                    tc = tcp * 2 + tci
                    pbank = banks[tci]
                    w0 = [wtok, S[f"gh_w_{hc}"]]
                    if hc == 0:
                        w0.append(pWoR[0] if tci == 0 else pFR[0])
                    t1 = K.op("tensor", "matmul", pbank[:, 0:TT],
                              ghb[:, hc, tc * P:(tc + 1) * P],
                              wo_b[:, pbuf, 0:TT],
                              start=(hc == 0), stop=(hc == HC - 1),
                              waits=w0, inc=0)
                    t2 = K.op("tensor", "matmul", pbank[:, TT:D],
                              ghb[:, hc, tc * P:(tc + 1) * P],
                              wo_b[:, pbuf, TT:D],
                              start=(hc == 0), stop=(hc == HC - 1),
                              inc=1 if (tci == 1 or hc == HC - 1) else 0)
                    if hc == HC - 1:
                        if tci == 0:
                            pWoW[0] = t2
                        else:
                            pFW[0] = t2
                        S["wo_mm"][tc] = t2
                    if tci == 1:
                        S["wo_read"][pbuf] = t2
            for tci in range(2):
                tc = tcp * 2 + tci
                emit_back_s(pg, tc, banks[tci])

        def emit_tile(g):
            seq_i = (g * TT) // SEQ
            toff = (g * TT) % SEQ
            seq_start = (toff == 0)
            pg = g - 1

            xt_toks = []
            for dc in range(DC):
                sl = dc % 3
                stg = stg32[:, sl * TT:(sl + 1) * TT]
                w = [S.get("xTb_read"), S.get(f"stgslot_{sl}"),
                     S.get("stg_read")]
                lx = K.op("sync", "dma_start", out=stg,
                          in_=xt[seq_i, dc * P:(dc + 1) * P, toff:toff + TT],
                          waits=w, dma=True, sem=f"ld_stg{sl}")
                cx = K.op("scalar", "copy", out=xTb[:, dc, :],
                          in_=stg, waits=[lx] + w)
                S[f"stgslot_{sl}"] = cx
                if g == 0:
                    S["stg_read"] = cx
                xt_toks.append(cx)
            if seq_start:
                cm = K.op("vector", "memset", carry[:], 0.0,
                          waits=S["carry_tok"])
                tm = K.op("vector", "memset", tailb[:], 0.0,
                          waits=S["tail_tok"])
                S["carry_tok"] = [cm] * HC
                S["tail_tok"] = [tm] * HC

            use_cache = (g > 0) and not os.environ.get("KM_NO_CACHE")
            S["wg_pre"] = {}

            m_order = list(range(HC, GC)) + list(range(0, HC))
            conv_rot = 0
            for mi, m in enumerate(m_order):
                pbuf = mi % 3
                wtok = load_piece(
                    "wi", m, f"ld_wi{pbuf}", wi_b[:, pbuf], wiC,
                        lambda m=m: wiT.rearrange("(k p) h -> p k h", p=P)[:, :, m * P:(m + 1) * P],
                    S["wi_read"][pbuf], use_cache)
                bank = mi % 3
                mm_waits = [wtok, pbfR[bank]]
                if mi == 0:
                    mm_waits += xt_toks
                tmm = None
                for k in range(DC):
                    tmm = K.op("tensor", "matmul", pbf[bank][:],
                               wi_b[:, pbuf, k, :], xTb[:, k, :],
                               start=(k == 0), stop=(k == DC - 1),
                               waits=mm_waits if k == 0 else [],
                               inc=0 if k < DC - 1 else 1)
                pbfW[bank] = tmm
                S["wi_read"][pbuf] = tmm
                if mi == len(m_order) - 1:
                    S["xTb_read"] = tmm
                if m >= HC:
                    c = m - HC
                    cr = conv_rot % 2
                    conv_rot += 1
                    scr = convs[:, cr, :]
                    acc = convacc[:, 0, :]
                    wv = [S["conv_read"][cr]] if S["conv_read"][cr] else []
                    t_lead = K.op("vector", "tensor_copy", out=scr[:, 0:3],
                                  in_=tailb[:, c, :],
                                  waits=wv + [S["tail_tok"][c]])
                    t_scr = K.op("scalar", "copy", out=scr[:, 3:3 + TT],
                                 in_=pbf[bank][:], waits=[tmm] + wv)
                    wv2 = [S["convacc_read"][0]] if S["convacc_read"][0] else []
                    t_acc = K.op("scalar", "activation", out=acc,
                                 in_=pbf[bank][:], func=AF.Identity,
                                 scale=cw[:, 3 * HC + c:3 * HC + c + 1],
                                 bias=cb[:, c:c + 1],
                                 waits=[tmm, l_cw, l_cb] + wv2)
                    pbfR[bank] = t_acc
                    for kk in (2, 1, 0):
                        if kk != 0:
                            dst = acc
                            wv3 = [t_scr, t_acc, t_lead]
                        else:
                            dst = xh[:, c, :]
                            wv3 = [t_scr, t_acc, t_lead]
                            if S["xh_read"][c]:
                                wv3.extend(S["xh_read"][c])
                        t_acc = K.op("vector", "scalar_tensor_tensor",
                                     out=dst, in0=scr[:, kk:kk + TT],
                                     scalar=cw[:, kk * HC + c:kk * HC + c + 1],
                                     in1=acc, op0=ALU.mult, op1=ALU.add,
                                     waits=wv3)
                    S["convacc_read"][0] = t_acc
                    S[f"xh_w_{c}"] = t_acc
                    t_tail = K.op("vector", "tensor_copy", out=tailb[:, c, :],
                                  in_=scr[:, TT:TT + 3], waits=[t_scr])
                    S["tail_tok"][c] = t_tail
                    S["conv_read"][cr] = t_tail
                else:
                    c = m
                    wv = [S["gg_read"][c]] if S["gg_read"][c] else []
                    t_g = K.op("scalar", "activation", out=gg[:, c, :],
                               in_=pbf[bank][:], func=AF.Gelu,
                               waits=[tmm] + wv)
                    pbfR[bank] = t_g
                    S[f"gg_w_{c}"] = t_g
                if g > 0 and mi == 5:
                    S["s_toks"] = []
                    emit_wo_pair(pg, 0, pg > 0 and not os.environ.get("KM_NO_CACHE"))
                if g > 0 and mi == 11:
                    emit_wo_pair(pg, 1, not os.environ.get("KM_NO_CACHE"))
                    for gi in range(HC):
                        S["ghb_read"][gi] = K.milestone("tensor")
                    S["have_back"] = True
                    S["xtok_read"] = list(S["s_toks"])
                if g > 0 and mi in (13, 15, 17, 19):
                    emit_back_stats((mi - 13) // 2)
                if g == 0 and mi in (4, 6, 8, 10, 12, 14, 16, 18):
                    pump_fw()
                if 12 <= mi <= 14:
                    i6 = mi - 12
                    pgi, phalf = divmod(i6, 2)
                    m2 = pgi if phalf == 0 else HC + pgi
                    pb = i6 % 3
                    S["wg_pre"][(pgi, phalf)] = load_piece(
                        "wg", m2, f"ld_wg{pb}", wg_b[:, pb], wgC,
                        lambda m2=m2: wgT.rearrange("(k p) h -> p k h", p=P)[:, :, m2 * P:(m2 + 1) * P],
                        S["wg_read"][pb], use_cache)

            xh_all = [S[f"xh_w_{c}"] for c in range(HC)]
            l_xtok = K.op("scalar", "dma_start", out=xtok[:],
                          in_=xtok_d.rearrange("(n c p) d -> n p c d",
                                               c=TC, p=P)[g],
                          waits=S["xtok_read"], dma=True, sem="ld_xtok")
            S["l_xtok"] = l_xtok
            S["xtok_read"] = []

            has_prev = bool(S.get("have_back"))
            wg_ap = [pbf[0][:], pbf[1][:], pbf[2][:],
                     pWo[:, 0:TT], pWo[:, TT:2 * TT]]
            pbfR[3] = pbfR[4] = pWoR[0]
            rot = 0
            for gi in range(HC):
                if g == 0 and gi == 10:
                    emit_wd_transposes()
                grp_i = gi % GRP
                halves = []
                for half, m in ((0, gi), (1, HC + gi)):
                    pbuf = (2 * gi + half) % 3
                    if (gi, half) in S["wg_pre"]:
                        wtok = S["wg_pre"].pop((gi, half))
                    else:
                        wtok = load_piece(
                            "wg", m, f"ld_wg{pbuf}", wg_b[:, pbuf], wgC,
                            lambda m=m: wgT.rearrange("(k p) h -> p k h", p=P)[:, :, m * P:(m + 1) * P],
                            S["wg_read"][pbuf], use_cache)
                    bank = rot % 5
                    rot += 1
                    mm_waits = [wtok, pbfR[bank]]
                    if gi == 0 and half == 0:
                        mm_waits += xh_all
                    tmm = None
                    for k in range(HC):
                        tmm = K.op("tensor", "matmul", wg_ap[bank],
                                   wg_b[:, pbuf, k, :], xh[:, k, :],
                                   start=(k == 0), stop=(k == HC - 1),
                                   waits=mm_waits if k == 0 else [],
                                   inc=0 if k < HC - 1 else 1)
                    pbfW[bank] = tmm
                    S["wg_read"][pbuf] = tmm
                    halves.append((bank, tmm))
                bank_f, tok_f = halves[0]
                bank_i, tok_i = halves[1]
                ascr = convs[:, gi % 2, 0:TT]
                _ar = S["abc_read"][grp_i]
                wv = ([] if not _ar else (_ar if isinstance(_ar, list) else [_ar]))
                wva = [S[f"ascr_read_{gi % 2}"]] if S.get(f"ascr_read_{gi % 2}") else [S["conv_read"][gi % 2]]
                t_tf = K.op("scalar", "activation", out=bufA[:, grp_i, :],
                            in_=wg_ap[bank_f], func=AF.Tanh, scale=0.5,
                            bias=bgh[:, gi:gi + 1],
                            waits=[tok_f, t_bgh] + wv)
                pbfR[bank_f] = t_tf
                t_al = K.op("scalar", "activation", out=bufA[:, grp_i, :],
                            in_=bufA[:, grp_i, :], func=AF.Exp,
                            scale=chalf[:, gi:gi + 1],
                            bias=chalf[:, gi:gi + 1],
                            waits=[t_tf, t_ch])
                t_asq = K.op("scalar", "activation", out=ascr,
                             in_=bufA[:, grp_i, :], func=AF.Square,
                             waits=[t_al] + wva)
                t_bsq = K.op("vector", "tensor_scalar", out=bufC[:, grp_i, :],
                             in0=ascr, scalar1=-1.0,
                             scalar2=1.0 + EPS_B, op0=ALU.mult, op1=ALU.add,
                             waits=[t_asq] + wv)
                S[f"ascr_read_{gi % 2}"] = t_bsq
                t_ti = K.op("scalar", "activation", out=bufB[:, grp_i, :],
                            in_=wg_ap[bank_i], func=AF.Tanh, scale=0.5,
                            bias=bgh[:, HC + gi:HC + gi + 1],
                            waits=[tok_i, t_bgh] + wv)
                pbfR[bank_i] = t_ti
                t_xp = K.op("vector", "scalar_tensor_tensor",
                            out=bufB[:, grp_i, :], in0=bufB[:, grp_i, :],
                            scalar=1.0, in1=xh[:, gi, :],
                            op0=ALU.add, op1=ALU.mult,
                            waits=[t_ti, xh_all[gi]])
                S[f"chain_{gi}"] = (t_bsq, t_xp, t_al)
                if has_prev:
                    if gi == 6:
                        emit_back_fin(0)
                    if gi == 8:
                        emit_back_fin(1)
                    if gi == 10:
                        emit_back_fin(2)
                        S["fin2_done"] = True
                if grp_i == GRP - 1:
                    emit_scan_group(g, gi - GRP + 1,
                                    with_sd=(has_prev and gi == GRP - 1))

            if has_prev:
                emit_ffn_prev(g)

            wg_done = K.milestone("tensor")
            for c in range(HC):
                S["xh_read"][c] = [S[f"gh_w_{c}"], wg_done]
            pwr = [t for t in (pbfR[3], pbfR[4]) if t]
            if pwr:
                pWoR[0] = max(pwr, key=lambda t: t[1])

            S["prev_g"] = g
            for j in (0, 1):
                if S.get(f"ascr_read_{j}"):
                    S["conv_read"][j] = S[f"ascr_read_{j}"]

        dbg = {}
        if debug:
            dbg["d_xh"] = nc.dram_tensor("d_xh", [P, HC, TT], BF16, kind="ExternalOutput").ap()
            dbg["d_gg"] = nc.dram_tensor("d_gg", [P, HC, TT], BF16, kind="ExternalOutput").ap()
            dbg["d_ghb"] = nc.dram_tensor("d_ghb", [P, HC, TT], BF16, kind="ExternalOutput").ap()
            dbg["d_y"] = nc.dram_tensor("d_y", [P, TC, D], F32, kind="ExternalOutput").ap()
            dbg["d_xd"] = nc.dram_tensor("d_xd", [P, TC, D], BF16, kind="ExternalOutput").ap()
            dbg["d_xdf"] = nc.dram_tensor("d_xdf", [P, DC, TT], BF16, kind="ExternalOutput").ap()
            dbg["d_wdT"] = nc.dram_tensor("d_wdT", [P, DC, D], BF16, kind="ExternalOutput").ap()
            dbg["d_woC"] = nc.dram_tensor("d_woC", [HC, P, D], BF16, kind="ExternalOutput").ap()
            dbg["d_wiC"] = nc.dram_tensor("d_wiC", [GC, P, DC * P], BF16, kind="ExternalOutput").ap()
            dbg["d_wgC"] = nc.dram_tensor("d_wgC", [GC, P, HC * P], BF16, kind="ExternalOutput").ap()
        def flush_back(n_tiles):
            S["s_toks"] = []
            emit_wo_pair(n_tiles - 1, 0,
                         n_tiles > 1 and not os.environ.get("KM_NO_CACHE"))
            emit_wo_pair(n_tiles - 1, 1, not os.environ.get("KM_NO_CACHE"))
            for tc in range(TC):
                emit_back_stats(tc)
            for tc in range(TC):
                emit_back_sd(tc)
            emit_back_fin(0)
            emit_back_fin(1)
            S["have_back"] = True
            emit_ffn_prev(n_tiles)

        for g in range(n_tiles):
            emit_tile(g)
            if debug and g == 0:
                for nm, sbuf in [("d_xh", xh), ("d_gg", gg), ("d_ghb", ghb),
                                 ("d_y", y), ("d_xd", xd_tok), ("d_wdT", wdT)]:
                    K.op("scalar", "dma_start", out=dbg[nm], in_=sbuf[:],
                         waits=[K.milestone("vector"), K.milestone("scalar")],
                         dma=True, sem="st_dbg")
        flush_back(n_tiles)
        if debug:
            K.op("scalar", "dma_start", out=dbg["d_xdf"], in_=xd_feat[:],
                 waits=[K.milestone("scalar"), K.milestone("tensor")], dma=True,
                 sem="st_dbg")
            K.op("scalar", "dma_start", out=dbg["d_woC"], in_=woC,
                 waits=[K.milestone("scalar")], dma=True, sem="st_dbg")
            K.op("scalar", "dma_start", out=dbg["d_wiC"], in_=wiC,
                 waits=[K.fence("st_wi")], dma=True, sem="st_dbg")
            K.op("scalar", "dma_start", out=dbg["d_wgC"], in_=wgC,
                 waits=[K.fence("st_wg")], dma=True, sem="st_dbg")

        K.wait("sync", [K.milestone(e) for e in ENGINES]
               + [K.fence(ch) for ch in list(K.cval.keys())])

        K.build(nc, stack, simsafe=bool(os.environ.get("BASS_SIM_FP64"))
                or bool(os.environ.get("KM_SIMSAFE")))
    return nc


_CACHE = {}


def _host_prep(L0_attn_x, L0_attn_Wi, L0_attn_conv_w, L0_attn_conv_b,
               L0_attn_Wg, L0_attn_bg, L0_attn_forget_base, L0_attn_Wo,
               L0_ffn_weight, L0_ffn_bias):
    x = np.ascontiguousarray(np.asarray(L0_attn_x, dtype=np.float32))
    Wi = np.ascontiguousarray(np.asarray(L0_attn_Wi, np.float32))
    Wg = np.ascontiguousarray(np.asarray(L0_attn_Wg, np.float32))
    Wo = np.ascontiguousarray(np.asarray(L0_attn_Wo, np.float32))
    cwv = np.asarray(L0_attn_conv_w, np.float32)
    cbv = np.asarray(L0_attn_conv_b, np.float32)
    fbv = np.asarray(L0_attn_forget_base, np.float32)
    bgv = np.asarray(L0_attn_bg, np.float32)
    fw = np.ascontiguousarray(np.asarray(L0_ffn_weight, np.float32))
    fbias = np.ascontiguousarray(np.asarray(L0_ffn_bias, np.float32).reshape(1, D))

    wiT = np.ascontiguousarray(Wi.T)
    wgT = np.ascontiguousarray(Wg.T)
    woT = np.ascontiguousarray(Wo.T)
    cw_h = np.ascontiguousarray(
        cwv.reshape(HC, P, 4).transpose(1, 2, 0).reshape(P, 4 * HC))
    cb_h = np.ascontiguousarray(cbv.reshape(HC, P).T)
    fb_h = np.ascontiguousarray(fbv.reshape(HC, P).T)
    bg_h = np.ascontiguousarray(bgv.reshape(2 * HC, P).T)
    identb = np.eye(P, dtype=ml_dtypes.bfloat16)

    shared = dict(wiT=wiT, wgT=wgT, woT=woT, cw=cw_h, cb=cb_h, fb=fb_h,
                  bg=bg_h, fw=fw, fbias=fbias, identb=identb)
    in_maps = []
    for core in range(8):
        xc = x[2 * core:2 * core + 2]
        xt = np.ascontiguousarray(xc.transpose(0, 2, 1))
        xtok = np.ascontiguousarray(xc.reshape(NSEQ * SEQ, D))
        m = dict(shared)
        m["xt"] = xt
        m["xtok"] = xtok
        in_maps.append(m)
    return in_maps


def kernel(**inputs):
    if "nc" not in _CACHE:
        _CACHE["nc"] = build_nc()
    nc = _CACHE["nc"]
    in_maps = _host_prep(**inputs)
    res = run_bass_kernel_spmd(nc, in_maps, core_ids=list(range(8)))
    outs = [np.asarray(r["out"]).reshape(NSEQ, SEQ, D) for r in res.results]
    return np.concatenate(outs, axis=0)

